# version 11
# baseline (speedup 1.0000x reference)
import numpy as np
import concourse.bass as bass
import concourse.mybir as mybir
from concourse.bass_utils import run_bass_kernel_spmd

F32 = mybir.dt.float32
BF16 = mybir.dt.bfloat16
AF = mybir.ActivationFunctionType
ALU = mybir.AluOpType
AX = mybir.AxisListType.X

ENGS = ("sync", "tensor", "scalar", "vector", "gpsimd")

SQRT_HALF = 0.70710678118654752


class Sched:

    def __init__(self):
        self.by_eng = {e: [] for e in ENGS}
        self.ccount = {}
        self.ops = []
        self.res = {}

    def add(self, eng, fn, reads=(), writes=(), dma=False, tag=""):
        cs = ("q_" + eng) if dma else eng
        cidx = self.ccount.get(cs, 0)
        self.ccount[cs] = cidx + 1
        op = {"eng": eng, "fn": fn, "dma": dma, "cs": cs, "cidx": cidx,
              "deps": {}, "id": len(self.ops), "tag": tag}

        def isbank(r):
            return isinstance(r, tuple) and r and r[0] == "bank"

        def dep(d, bank):
            op["deps"][d] = op["deps"].get(d, False) or bank

        for r in reads:
            e = self.res.get(r)
            if e and e["w"] is not None:
                dep(e["w"], isbank(r))
        for w in writes:
            e = self.res.get(w)
            if e:
                if e["w"] is not None:
                    dep(e["w"], isbank(w))
                for rr in e["r"]:
                    dep(rr, isbank(w))
        self.ops.append(op)
        self.by_eng[eng].append(op)
        for r in reads:
            self.res.setdefault(r, {"w": None, "r": set()})["r"].add(op["id"])
        for w in writes:
            self.res[w] = {"w": op["id"], "r": set()}
        return op["id"]

    @staticmethod
    def _skip(dop, op, bank=False):
        if (not dop["dma"]) and dop["eng"] == op["eng"]:
            if bank and not (dop["eng"] == "tensor" and op["eng"] == "tensor"):
                return False
            return True
        return False

    def finalize(self):
        marks = {}
        for op in self.ops:
            if op["dma"]:
                marks.setdefault(op["cs"], set()).add(op["cidx"])
            for d, bank in op["deps"].items():
                dop = self.ops[d]
                if self._skip(dop, op, bank):
                    continue
                marks.setdefault(dop["cs"], set()).add(dop["cidx"])
        self.pref = {}
        for cs, s in marks.items():
            m = {}
            for i, ci in enumerate(sorted(s)):
                m[ci] = i + 1
            self.pref[cs] = m
        self.marked = set()
        for op in self.ops:
            m = self.pref.get(op["cs"])
            if m and op["cidx"] in m:
                self.marked.add(op["id"])

    def emit_engine(self, ename, eng_obj, sems):
        waited = {}
        for op in self.by_eng[ename]:
            tgts = {}
            for d, bank in op["deps"].items():
                dop = self.ops[d]
                if self._skip(dop, op, bank):
                    continue
                cs = dop["cs"]
                amt = 16 if dop["dma"] else 1
                val = self.pref[cs][dop["cidx"]] * amt
                if tgts.get(cs, 0) < val:
                    tgts[cs] = val
            if op["dma"]:
                qcs = op["cs"]
                want = 16 * op["cidx"]
                if want > 0 and tgts.get(qcs, 0) < want:
                    tgts[qcs] = want
            for cs in sorted(tgts):
                val = tgts[cs]
                if waited.get(cs, 0) < val:
                    eng_obj.wait_ge(sems[cs], val)
                    waited[cs] = val
            inst = op["fn"](eng_obj)
            if op["id"] in self.marked:
                inst.then_inc(sems[op["cs"]], 16 if op["dma"] else 1)


def build(stage="full"):
    nc = bass.Bass("TRN2", target_bir_lowering=False)
    sc = Sched()

    dx = nc.dram_tensor("x", [128, 8, 2048], F32, kind="ExternalInput").ap()
    dwrF = nc.dram_tensor("wrF", [128, 8, 1024], F32, kind="ExternalInput").ap()
    dwkF = nc.dram_tensor("wkF", [128, 8, 1024], F32, kind="ExternalInput").ap()
    dwvF = nc.dram_tensor("wvF", [128, 8, 1024], F32, kind="ExternalInput").ap()
    dwq = nc.dram_tensor("wq", [2, 128, 8, 4, 128], F32, kind="ExternalInput").ap()
    dwkA = nc.dram_tensor("wkA", [2, 128, 8, 512], F32, kind="ExternalInput").ap()
    dwvA = nc.dram_tensor("wvA", [2, 128, 8, 512], F32, kind="ExternalInput").ap()
    dwo = nc.dram_tensor("wo", [2, 64, 8, 8, 128], F32, kind="ExternalInput").ap()
    dmur = nc.dram_tensor("mur", [128, 8], F32, kind="ExternalInput").ap()
    dmuk = nc.dram_tensor("muk", [128, 8], F32, kind="ExternalInput").ap()
    dbq = nc.dram_tensor("bq", [128, 8], F32, kind="ExternalInput").ap()
    dbk = nc.dram_tensor("bk", [128, 8], F32, kind="ExternalInput").ap()
    dbo = nc.dram_tensor("bo", [128, 8], F32, kind="ExternalInput").ap()
    dlnw = nc.dram_tensor("lnw", [128, 8], F32, kind="ExternalInput").ap()
    dlnb = nc.dram_tensor("lnb", [128, 8], F32, kind="ExternalInput").ap()
    dbv = nc.dram_tensor("bv", [1, 1024], F32, kind="ExternalInput").ap()
    dtri = nc.dram_tensor("tri", [128, 128], F32, kind="ExternalInput").ap()
    dout = nc.dram_tensor("out", [128, 8, 2048], F32, kind="ExternalOutput").ap()
    dscr = nc.dram_tensor("scr", [5, 1024], F32, kind="Internal").ap()

    ctx = []

    def sb(name, shape, dt):
        g = nc.sbuf_tensor(name, shape, dt)
        t = g.__enter__()
        ctx.append(g)
        return t

    xS = sb("xS", [128, 8, 2048], F32)
    pA = sb("pA", [128, 8, 2048], BF16)
    pB = sb("pB", [128, 16512], BF16)
    pC = sb("pC", [128, 8, 512], BF16)
    wA = sb("wAt", [128, 8, 1024], BF16)
    wB = sb("wBt", [128, 8192], BF16)
    rtmp = sb("rtmp", [128, 2048], F32)
    rowA = sb("rowA", [1, 512], F32)
    rowB = sb("rowB", [1, 512], F32)
    rowC = sb("rowC", [1, 16], F32)
    kns = sb("kns", [1, 16], F32)
    l64 = sb("l64", [65, 1024], F32)
    lbsb = sb("lbsb", [64, 1024], F32)
    nbC = sb("nbC", [128, 4], F32)
    murC = sb("murC", [128, 8], F32)
    mukC = sb("mukC", [128, 8], F32)
    murI = sb("murI", [128, 8], F32)
    mukI = sb("mukI", [128, 8], F32)
    bqC = sb("bqC", [128, 8], F32)
    bkC = sb("bkC", [128, 8], F32)
    boC = sb("boC", [128, 8], F32)
    lnwC = sb("lnwC", [128, 8], F32)
    lnbC = sb("lnbC", [128, 8], F32)
    bvB = sb("bvB", [1, 1024], BF16)
    triB = sb("triB", [128, 128], BF16)
    onesB = sb("onesB", [128, 128], BF16)
    oTop = sb("oTop", [128, 1], BF16)
    oBot = sb("oBot", [128, 1], BF16)
    consts = sb("consts", [128, 12], F32)
    onesFc = sb("onesFc", [128, 1], F32)
    wqh = sb("wqh", [128, 8, 4, 128], BF16)
    psg = nc.psum_tensor("ps", [128, 8, 512], F32)
    ps = psg.__enter__()
    ctx.append(psg)

    pC3 = pC
    pBv = pB[:, 0:16384].rearrange("p (c t) -> p c t", c=8)
    Kv = pB[:, 0:8192].rearrange("p (a t) -> p a t", a=4)
    Vv = pB[:, 8192:16512].rearrange("p (t h e) -> p t h e", t=16, h=8)
    rbufB = rtmp[:].bitcast(BF16).rearrange("p (a n) -> p a n", a=8)
    zv = wB[:, 4096:8192].rearrange("p (h n) -> p h n", h=8)
    wBq = wB[:, 0:2048].rearrange("p (q n) -> p q n", q=4)
    wBf = wB[:].rearrange("p (c d) -> p c d", c=8)
    wAf = wA
    wkhA = wA[:].rearrange("p c (x n) -> p (c x) n", x=2)[:, 0:8, :]
    wvhA = wA[:].rearrange("p c (x n) -> p (c x) n", x=2)[:, 8:16, :]
    wqv = pC[:, 6:8, :].rearrange("p a n -> p (a n)").rearrange("p (c m) -> p c m", c=8)

    def pBr(lo, hi):
        return ["pB%d" % s for s in range(lo // 2048, (hi + 2047) // 2048)]

    K_CONST = ["consts"]
    WA_ALL = [("wA", c) for c in range(8)]
    WB_ALL = [("wB", c) for c in range(8)]

    def MM(e, out, lhsT, rhs, start, stop, tp=None):
        return nc.tensor.matmul(out, lhsT, rhs, start=start, stop=stop,
                                tile_position=tp, skip_group_check=True)

    CONSTVALS = [(-1.0, 0), (0.5, 1), (SQRT_HALF, 2), (1.0 / 1024.0, 3), (0.125, 4),
                 (-0.5, 5), (1e-5, 6), (-0.05, 7), (-1.02, 8), (2.0, 9), (-1e30, 10)]

    def init_consts():
        def f(v, col):
            return lambda e, v=v, col=col: e.memset(consts[:, col:col + 1], v)
        for v, col in CONSTVALS:
            sc.add("vector", f(v, col), writes=K_CONST)
            nc.const_aps.register(F32, v, consts[:, col:col + 1])
        sc.add("vector", lambda e: e.memset(onesB[:], 1.0), writes=K_CONST)
        sc.add("vector", lambda e: e.memset(onesFc[:], 1.0), writes=K_CONST)
        sc.add("vector", lambda e: e.memset(oTop[0:64, :], 1.0), writes=K_CONST)
        sc.add("vector", lambda e: e.memset(oTop[64:128, :], 0.0), writes=K_CONST)
        sc.add("vector", lambda e: e.memset(oBot[0:64, :], 0.0), writes=K_CONST)
        sc.add("vector", lambda e: e.memset(oBot[64:128, :], 1.0), writes=K_CONST)
        for en in ("tensor", "scalar", "gpsimd", "sync"):
            sc.add(en, lambda e: e.nop(), reads=K_CONST)

    def init_loads():
        sc.add("sync", lambda e: e.dma_start(out=xS[:, 0:4, :], in_=dx[:, 0:4, :]),
               writes=[("x", c, t) for c in range(4) for t in range(4)],
               dma=True, tag="ldx")
        sc.add("sync", lambda e: e.dma_start(out=xS[:, 4:8, :], in_=dx[:, 4:8, :]),
               writes=[("x", c, t) for c in range(4, 8) for t in range(4)],
               dma=True, tag="ldx")
        for dst, src, nm in ((murC, dmur, "murC"), (mukC, dmuk, "mukC"),
                             (bqC, dbq, "bqC"), (bkC, dbk, "bkC"), (boC, dbo, "boC"),
                             (lnwC, dlnw, "lnwC"), (lnbC, dlnb, "lnbC")):
            sc.add("sync", lambda e, d=dst, s=src: e.dma_start(out=d[:], in_=s[:]),
                   writes=[nm], dma=True)
        sc.add("gpsimd", lambda e: e.dma_start(out=bvB[:], in_=dbv[:]), writes=["bvB"], dma=True)
        sc.add("gpsimd", lambda e: e.dma_start(out=triB[:], in_=dtri[:]), writes=["triB"], dma=True)
        sc.add("scalar", lambda e: e.mul(bqC[:], bqC[:], 0.125),
               reads=K_CONST + ["bqC"], writes=["bqC"])
        sc.add("vector", lambda e: e.tensor_scalar(murI[:], murC[:], -1.0, 1.0, ALU.mult, ALU.add),
               reads=K_CONST + ["murC"], writes=["murI"])
        sc.add("vector", lambda e: e.tensor_scalar(mukI[:], mukC[:], -1.0, 1.0, ALU.mult, ALU.add),
               reads=K_CONST + ["mukC"], writes=["mukI"])

    def ln(kind, tag):
        apply_pend = []
        for t in range(4):
            tsl0 = slice(512 * t, 512 * (t + 1))
            for c in range(8):
                reg = pBr(2048 * c, 2048 * (c + 1))
                sc.add("tensor",
                       lambda e, c=c, t=t, tsl0=tsl0: MM(e, ps[0:1, t, :], onesFc[:, 0:1],
                                                         xS[:, c, tsl0],
                                                         c == 0, c == 7),
                       reads=[("x", c, t)],
                       writes=[("bank", t)] if c in (0, 7) else [],
                       tag=tag + "s1")
                sc.add("vector", lambda e, c=c, tsl0=tsl0: e.tensor_mul(
                    pBv[:, c, tsl0], xS[:, c, tsl0], xS[:, c, tsl0]),
                       reads=[("x", c, t)] + reg, writes=reg, tag=tag + "sq")
                sc.add("tensor",
                       lambda e, c=c, t=t, tsl0=tsl0: MM(e, ps[0:1, 4 + t, :], onesB[:, 0:1],
                                                         pBv[:, c, tsl0],
                                                         c == 0, c == 7),
                       reads=reg,
                       writes=[("bank", 4 + t)] if c in (0, 7) else [],
                       tag=tag + "s2")
            s1 = ps[0:1, t, :]
            s2 = ps[0:1, 4 + t, :]
            bt, b4t = ("bank", t), ("bank", 4 + t)
            sc.add("scalar", lambda e, s1=s1: e.mul(s1, s1, 1.0 / 1024.0),
                   reads=[bt] + K_CONST, writes=[bt], tag=tag + "mean")
            sc.add("scalar", lambda e, s2=s2: e.mul(s2, s2, 1.0 / 1024.0),
                   reads=[b4t] + K_CONST, writes=[b4t], tag=tag + "msq")
            sc.add("scalar", lambda e, s1=s1: e.activation(rowA[:], s1, AF.Square),
                   reads=[bt], writes=["rowA"])
            sc.add("vector", lambda e, s2=s2: e.tensor_sub(rowA[:], s2, rowA[:]),
                   reads=[b4t, "rowA"], writes=["rowA"])
            if kind == "plain":
                sc.add("scalar", lambda e: e.activation(rowA[:], rowA[:], AF.Ln,
                                                        bias=consts[0:1, 6:7]),
                       reads=["rowA"] + K_CONST, writes=["rowA"])
                sc.add("scalar", lambda e: e.activation(rowA[:], rowA[:], AF.Exp, scale=-0.5),
                       reads=["rowA"], writes=["rowA"])
            else:
                sc.add("scalar", lambda e: e.activation(rowB[:], rowA[:], AF.Ln,
                                                        bias=consts[0:1, 6:7]),
                       reads=["rowA"] + K_CONST, writes=["rowB"])
                sc.add("scalar", lambda e: e.activation(l64[0:1, 0:512], rowB[:], AF.Exp,
                                                        scale=-1.0),
                       reads=["rowB"], writes=["l64"])
                sc.add("vector", lambda e: e.tensor_mul(l64[0:1, 0:512], rowA[:],
                                                        l64[0:1, 0:512]),
                       reads=["rowA", "l64"], writes=["l64"])
                sc.add("scalar", lambda e: e.activation(l64[0:1, 0:512], l64[0:1, 0:512],
                                                        AF.Ln, bias=consts[0:1, 6:7]),
                       reads=["l64"] + K_CONST, writes=["l64"])
                sc.add("vector", lambda e: e.tensor_add(rowB[:], rowB[:], l64[0:1, 0:512]),
                       reads=["rowB", "l64"], writes=["rowB"])
                sc.add("scalar", lambda e: e.activation(rowA[:], rowB[:], AF.Exp, scale=-0.5),
                       reads=["rowB"], writes=["rowA"])
            sc.add("vector", lambda e, s1=s1: e.scalar_tensor_tensor(
                rowB[:], s1, -1.0, rowA[:], ALU.mult, ALU.mult),
                   reads=[bt, "rowA"] + K_CONST, writes=["rowB"])
            half = 1024 * (t % 2)
            scr_sl = 2 if t % 2 == 0 else 4
            rt_names = ["rtQ0", "rtQ1"] if t % 2 == 0 else ["rtQ2", "rtQ3"]
            sc.add("sync", lambda e, scr_sl=scr_sl: e.dma_start(
                out=dscr[scr_sl:scr_sl + 1, 0:512], in_=rowA[:]),
                   reads=["rowA"], writes=[("scr", scr_sl)], dma=True, tag=tag + "bcAo")
            sc.add("sync", lambda e, scr_sl=scr_sl: e.dma_start(
                out=dscr[scr_sl:scr_sl + 1, 512:1024], in_=rowB[:]),
                   reads=["rowB"], writes=[("scr", scr_sl)], dma=True, tag=tag + "bcBo")
            sc.add("sync", lambda e, half=half, scr_sl=scr_sl: e.dma_start(
                out=rtmp[:, half:half + 1024],
                in_=dscr[scr_sl:scr_sl + 1, :].broadcast_to([128, 1024])),
                   reads=[("scr", scr_sl)], writes=rt_names, dma=True, tag=tag + "bcin")
            def mk_apply(t, half, rt_names):
                def emit():
                    for c in range(8):
                        tsl = slice(512 * t, 512 * (t + 1))
                        sc.add("vector", lambda e, c=c, tsl=tsl, half=half: e.tensor_mul(
                            pC3[:, c, :], xS[:, c, tsl], rtmp[:, half:half + 512]),
                               reads=[("x", c, t), rt_names[0]], writes=[("pC", c)],
                               tag=tag + "ap1")
                        if kind == "plain":
                            sc.add("vector", lambda e, c=c, tsl=tsl, half=half:
                                   e.tensor_add(
                                       pA[:, c, tsl], pC3[:, c, :],
                                       rtmp[:, half + 512:half + 1024]),
                                   reads=[("pC", c), rt_names[1]],
                                   writes=[("pA", c, t)], tag=tag + "ap2")
                        else:
                            sc.add("vector", lambda e, c=c, half=half: e.tensor_add(
                                pC3[:, c, :], pC3[:, c, :],
                                rtmp[:, half + 512:half + 1024]),
                                   reads=[("pC", c), rt_names[1]], writes=[("pC", c)],
                                   tag=tag + "ap2")
                            sc.add("vector", lambda e, c=c, tsl=tsl: e.tensor_scalar(
                                pA[:, c, tsl], pC3[:, c, :], lnwC[:, c:c + 1],
                                lnbC[:, c:c + 1], ALU.mult, ALU.add),
                                   reads=[("pC", c), "lnwC", "lnbC"],
                                   writes=[("pA", c, t)], tag=tag + "ap3")
                return emit
            apply_pend.append(mk_apply(t, half, rt_names))
            if len(apply_pend) > 1:
                apply_pend.pop(0)()
        for fn in apply_pend:
            fn()
        apply_pend = []
        if True:
            pass

    def ffn_weight_load(src, dst_view, dst_regs, tag):
        sc.add("gpsimd", lambda e, s=src, d=dst_view: e.dma_start(out=d, in_=s[:]),
               writes=dst_regs, dma=True, tag=tag)

    def mix(tag, t, mu, muI, muname, muIname):
        lo = 512 * t
        for c in range(8):
            if t == 0:
                sc.add("vector", lambda e, c=c: e.memset(pC3[:, c, 0:1], 0.0),
                       writes=[("pC", c)], tag=tag)
                sc.add("vector", lambda e, c=c, muI=muI: e.tensor_scalar(
                    pC3[:, c, 1:512], pA[:, c, 0:511], muI[:, c:c + 1], None, ALU.mult),
                       reads=[("pA", c, 0), muIname], writes=[("pC", c)], tag=tag)
            else:
                sc.add("vector", lambda e, c=c, lo=lo, muI=muI: e.tensor_scalar(
                    pC3[:, c, :], pA[:, c, lo - 1:lo + 511], muI[:, c:c + 1], None, ALU.mult),
                       reads=[("pA", c, t), ("pA", c, t - 1), muIname],
                       writes=[("pC", c)], tag=tag)
            sc.add("vector", lambda e, c=c, lo=lo, mu=mu: e.scalar_tensor_tensor(
                pC3[:, c, :], pA[:, c, lo:lo + 512], mu[:, c:c + 1], pC3[:, c, :],
                ALU.mult, ALU.add),
                   reads=[("pA", c, t), ("pC", c), muname], writes=[("pC", c)], tag=tag)

    def ffn(tag):
        kact = [pBr(2048 * d, 2048 * (d + 1)) for d in range(8)]
        for t in range(4):
            mix(tag + "mxk", t, mukC, mukI, "mukC", "mukI")
            for d in range(8):
                bidx = d % 4
                bank = ("bank", bidx)
                for c in range(8):
                    sc.add("tensor", lambda e, c=c, d=d, bidx=bidx: MM(
                        e, ps[:, bidx, :], wAf[:, c, 128 * d:128 * (d + 1)], pC3[:, c, :],
                        c == 0, c == 7),
                           reads=[("wA", c), ("pC", c)],
                           writes=[bank] if c in (0, 7) else [], tag=tag + "kmm")
                rh = slice(0, 512) if d % 2 == 0 else slice(512, 1024)
                rtr = ["rtQ0", "rtQ1"] if d % 2 == 0 else ["rtQ2", "rtQ3"]
                sc.add("scalar", lambda e, bidx=bidx, rh=rh: e.activation(
                    rtmp[:, rh], ps[:, bidx, :], AF.Relu),
                       reads=[bank], writes=rtr, tag=tag + "relu")
                sc.add("scalar", lambda e, d=d, t=t, rh=rh: e.activation(
                    pBv[:, d, 512 * t:512 * (t + 1)], rtmp[:, rh], AF.Square,
                    scale=SQRT_HALF),
                       reads=rtr, writes=kact[d], tag=tag + "sq")
        ffn_weight_load(dwvF, wAf[:, :, :], WA_ALL, tag + "ldwv")
        def emit_r(t, d):
            bidx = 2 + d % 2
            bank = ("bank", bidx)
            for c in range(8):
                sc.add("tensor", lambda e, c=c, d=d, bidx=bidx: MM(
                    e, ps[:, bidx, :], wBf[:, c, 128 * d:128 * (d + 1)], pC3[:, c, :],
                    c == 0, c == 7),
                       reads=[("wB", c), ("pC", c)],
                       writes=[bank] if c in (0, 7) else [], tag=tag + "rmm")
            rb = "rtQ%d" % (d % 2)
            sc.add("scalar", lambda e, bidx=bidx, d=d: e.activation(
                rbufB[:, d % 2, :], ps[:, bidx, :], AF.Sigmoid),
                   reads=[bank], writes=[rb], tag=tag + "sig")

        def emit_v(t, d):
            bidx = 4 + d % 2
            bank = ("bank", bidx)
            for c in range(8):
                sc.add("tensor", lambda e, c=c, d=d, t=t, bidx=bidx: MM(
                    e, ps[:, bidx, :], wAf[:, c, 128 * d:128 * (d + 1)],
                    pBv[:, c, 512 * t:512 * (t + 1)], c == 0, c == 7),
                       reads=[("wA", c)] + kact[c],
                       writes=[bank] if c in (0, 7) else [], tag=tag + "vmm")
            rb = "rtQ%d" % (d % 2)
            tb = "rtQ%d" % (2 + d % 2)
            sc.add("vector", lambda e, bidx=bidx, d=d: e.tensor_mul(
                rbufB[:, 2 + d % 2, :], ps[:, bidx, :], rbufB[:, d % 2, :]),
                   reads=[bank, rb], writes=[tb], tag=tag + "vmul")
            sc.add("vector", lambda e, d=d, t=t: e.tensor_add(
                xS[:, d, 512 * t:512 * (t + 1)], xS[:, d, 512 * t:512 * (t + 1)],
                rbufB[:, 2 + d % 2, :]),
                   reads=[tb, ("x", d, t)], writes=[("x", d, t)], tag=tag + "xadd")

        for t in range(4):
            mix(tag + "mxr", t, murC, murI, "murC", "murI")
            for d in range(8):
                emit_r(t, d)
                if d > 0:
                    emit_v(t, d - 1)
            emit_v(t, 7)

    def attention():
        for H in range(2):
            vregs = pBr(8192, 16512)
            kregs = [pBr(2048 * p, 2048 * (p + 1)) for p in range(4)]
            wk_regs = [("wA", s) for s in range(4)]
            wv_regs = [("wA", 4 + s) for s in range(4)]
            sc.add("gpsimd", lambda e, H=H: e.dma_start(out=wkhA[:, :, :], in_=dwkA[H]),
                   writes=wk_regs, dma=True, tag="ldwkA")
            sc.add("gpsimd", lambda e, H=H: e.dma_start(out=wvhA[:, :, :], in_=dwvA[H]),
                   writes=wv_regs, dma=True, tag="ldwvA")
            sc.add("gpsimd", lambda e, H=H: e.dma_start(out=wqh[:, :, :, :], in_=dwq[H]),
                   writes=[("wqh", p) for p in range(4)], dma=True, tag="ldwqh")
            sc.add("vector", lambda e: e.memset(Vv[:, :, :, 64:65], 1.0),
                   writes=vregs, tag="vones")
            for tt in range(16):
                bidx = tt % 4
                bank = ("bank", bidx)
                for c in range(8):
                    sc.add("tensor", lambda e, c=c, tt=tt, bidx=bidx: MM(
                        e, ps[:, bidx, :], pA[:, c, 128 * tt:128 * (tt + 1)], wvhA[:, c, :],
                        c == 0, False),
                           reads=[("pA", c, tt // 4), ("wA", 4 + c // 2)],
                           writes=[bank] if c == 0 else [], tag="vprojmm")
                sc.add("tensor", lambda e, H=H, bidx=bidx: MM(
                    e, ps[:, bidx, :], onesB[0:1, :], bvB[0:1, 512 * H:512 * (H + 1)],
                    False, True),
                       reads=["bvB"], writes=[bank], tag="vbias")
                sc.add("scalar", lambda e, tt=tt, bidx=bidx: e.copy(
                    Vv[:, tt, :, 0:64], ps[:, bidx, :].rearrange("p (h e) -> p h e", h=8)),
                       reads=[bank], writes=vregs, tag="vevac")
            sc.add("vector", lambda e: e.memset(kns[:, :], 0.0), writes=["kns"], tag="kns0")
            kn_pend = []
            for p in range(4):
                for t in range(4):
                    bidx = (p * 4 + t) % 4
                    bank = ("bank", bidx)
                    for c in range(8):
                        sc.add("tensor", lambda e, c=c, p=p, t=t, bidx=bidx: MM(
                            e, ps[:, bidx, :], wkhA[:, c, 128 * p:128 * (p + 1)],
                            pA[:, c, 512 * t:512 * (t + 1)], c == 0, c == 7),
                               reads=[("pA", c, t), ("wA", c // 2)],
                               writes=[bank] if c in (0, 7) else [], tag="kprojmm")
                    for fn in kn_pend:
                        fn()
                    kn_pend = []
                    sc.add("vector", lambda e, p=p, t=t, H=H, bidx=bidx: e.tensor_scalar(
                        Kv[:, p, 512 * t:512 * (t + 1)], ps[:, bidx, :],
                        bkC[:, 4 * H + p:4 * H + p + 1], None, ALU.add),
                           reads=[bank, "bkC"], writes=kregs[p], tag="kevac")
                    sc.add("vector", lambda e, p=p, t=t: e.tensor_mul(
                        rbufB[:, 0, :], Kv[:, p, 512 * t:512 * (t + 1)],
                        Kv[:, p, 512 * t:512 * (t + 1)]),
                           reads=kregs[p], writes=["rtQ0"], tag="k2")

                    def mk_kn(p, H):
                        def emit():
                            for h01 in range(2):
                                col = 2 * (4 * H + p) + h01
                                onesc = oTop if h01 == 0 else oBot
                                bnk = ("bank", 6 + h01)
                                sc.add("tensor", lambda e, h01=h01, onesc=onesc: MM(
                                    e, ps[0:1, 6 + h01, :], onesc[:, 0:1], rbufB[:, 0, :],
                                    True, True),
                                       reads=["rtQ0"] + K_CONST, writes=[bnk], tag="knmm")
                                sc.add("vector", lambda e, h01=h01, col=col: e.reduce_max(
                                    rowC[0:1, 3 + h01:4 + h01], ps[0:1, 6 + h01, :],
                                    axis=AX),
                                       reads=[bnk], writes=["rowC"], tag="knred")
                                sc.add("vector", lambda e, col=col, h01=h01: e.tensor_max(
                                    kns[0:1, col:col + 1], kns[0:1, col:col + 1],
                                    rowC[0:1, 3 + h01:4 + h01]),
                                       reads=["rowC", "kns"], writes=["kns"], tag="knmax")
                        return emit
                    kn_pend = [mk_kn(p, H)]
            for fn in kn_pend:
                fn()
            lo, hi = 8 * H, 8 * H + 8
            sc.add("scalar", lambda e, lo=lo, hi=hi: e.activation(
                kns[0:1, lo:hi], kns[0:1, lo:hi], AF.Ln, bias=consts[0:1, 6:7]),
                   reads=["kns"] + K_CONST, writes=["kns"], tag="knsln")
            sc.add("scalar", lambda e, lo=lo, hi=hi: e.activation(
                kns[0:1, lo:hi], kns[0:1, lo:hi], AF.Exp, scale=0.5),
                   reads=["kns"], writes=["kns"], tag="knsexp")

            if H == 1 and stage == "full":
                ffn_weight_load(dwkF, wAf[:, :, :], WA_ALL, "pf_wk2")

            def q_bounds(p, j):
                qslot = p
                qb = 4 + 2 * ((p + 1) % 2)
                bank6 = ("bank", qb)
                bank7 = ("bank", qb + 1)
                for c in range(8):
                    sc.add("tensor", lambda e, c=c, j=j, qb=qb, p=p: MM(
                        e, ps[:, qb, :], wqh[:, c, p, :], pA[:, c, 512 * j:512 * (j + 1)],
                        c == 0, c == 7),
                           reads=[("pA", c, j), ("wqh", p)],
                           writes=[bank6] if c in (0, 7) else [], tag="qmm")

            def q_bounds2(p, j):
                qslot = p
                qb = 4 + 2 * ((p + 1) % 2)
                bank6 = ("bank", qb)
                bank7 = ("bank", qb + 1)
                sc.add("vector", lambda e, p=p, H=H, qslot=qslot, qb=qb: e.tensor_scalar(
                    wBq[:, qslot, :], ps[:, qb, :], 0.125,
                    bqC[:, 4 * H + p:4 * H + p + 1], ALU.mult, ALU.add),
                       reads=[bank6, "bqC"] + K_CONST, writes=[("wBq", qslot)], tag="qevac")
                sc.add("vector", lambda e, qslot=qslot: e.tensor_mul(
                    rbufB[:, 4, :], wBq[:, qslot, :], wBq[:, qslot, :]),
                       reads=[("wBq", qslot)], writes=["rtQ2"], tag="q2")
                for h01 in range(2):
                    onesc = oTop if h01 == 0 else oBot
                    bnk = ("bank", qb + h01)
                    sc.add("tensor", lambda e, h01=h01, onesc=onesc, qb=qb: MM(
                        e, ps[0:1, qb + h01, :], onesc[:, 0:1], rbufB[:, 4, :],
                        True, True),
                           reads=["rtQ2"] + K_CONST,
                           writes=[bank6 if h01 == 0 else bank7], tag="qnmm")
                    sc.add("vector", lambda e, h01=h01, qb=qb: e.reduce_max(
                        rowC[0:1, h01:h01 + 1], ps[0:1, qb + h01, :], axis=AX),
                           reads=[("bank", qb + h01)], writes=["rowC"], tag="qnred")
                sc.add("scalar", lambda e: e.activation(rowC[0:1, 0:2], rowC[0:1, 0:2],
                                                        AF.Ln, bias=consts[0:1, 6:7]),
                       reads=["rowC"] + K_CONST, writes=["rowC"], tag="bln")
                sc.add("scalar", lambda e: e.activation(rowC[0:1, 0:2], rowC[0:1, 0:2],
                                                        AF.Exp, scale=0.5),
                       reads=["rowC"], writes=["rowC"], tag="bexp")
                hcols = 2 * (4 * H + p)
                sc.add("vector", lambda e, hcols=hcols: e.tensor_mul(
                    rowC[0:1, 0:2], rowC[0:1, 0:2], kns[0:1, hcols:hcols + 2]),
                       reads=["rowC", "kns"], writes=["rowC"], tag="bmul")
                sc.add("vector", lambda e: e.tensor_scalar(
                    rowC[0:1, 0:2], rowC[0:1, 0:2], -1.02, -0.05, ALU.mult, ALU.add),
                       reads=["rowC"] + K_CONST, writes=["rowC"], tag="bmargin")
                sc.add("vector", lambda e: e.tensor_reduce(
                    rowC[0:1, 2:3], rowC[0:1, 0:2], AX, ALU.min),
                       reads=["rowC"], writes=["rowC"], tag="bmin")
                sc.add("sync", lambda e, qslot=qslot: e.dma_start(
                    out=dscr[3:4, qslot:qslot + 1], in_=rowC[0:1, 2:3]),
                       reads=["rowC"], writes=[("scr3", qslot)], dma=True, tag="nbo")
                sc.add("sync", lambda e, qslot=qslot: e.dma_start(
                    out=nbC[:, qslot:qslot + 1],
                    in_=dscr[3:4, qslot:qslot + 1].broadcast_to([128, 1])),
                       reads=[("scr3", qslot)], writes=[("nb", qslot)], dma=True, tag="nbin")

            def emit_S(p, j, i, qslot):
                a = i % 2
                q0 = max(0, 128 * i - 512 * j)
                for h01 in range(2):
                    sbk = ("bank", 2 * a + h01)
                    sc.add("tensor", lambda e, p=p, i=i, h01=h01, q0=q0, a=a,
                           qslot=qslot: MM(
                               e, ps[:, 2 * a + h01, q0:512],
                               Kv[64 * h01:64 * h01 + 64, p, 128 * i:128 * (i + 1)],
                               wBq[64 * h01:64 * h01 + 64, qslot, q0:512],
                               True, True, tp=(64 * h01, 0)),
                           reads=[("wBq", qslot)] + kregs[p],
                           writes=[sbk], tag="smm")

            def s_iloop(p, j, hooks):
                qslot = p
                pvb = 4 + 2 * (p % 2)
                pvbank = [("bank", pvb), ("bank", pvb + 1)]
                imax = 4 * j + 3

                def emit_exp(i):
                    a = i % 2
                    q0 = max(0, 128 * i - 512 * j)
                    pslot = 2 + 2 * a
                    sc.add("scalar", lambda e, a=a, q0=q0, pslot=pslot, qslot=qslot:
                           e.activation(
                               pC3[:, pslot:pslot + 2, q0:512],
                               ps[:, 2 * a:2 * a + 2, q0:512],
                               AF.Exp, bias=nbC[:, qslot:qslot + 1]),
                           reads=[("bank", 2 * a), ("bank", 2 * a + 1), ("nb", qslot)],
                           writes=[("pC", pslot), ("pC", pslot + 1)], tag="exp")
                    if i >= 4 * j:
                        for h01 in range(2):
                            sc.add("vector", lambda e, pslot=pslot, h01=h01, q0=q0:
                                   e.tensor_mul(
                                       pC3[:, pslot + h01, q0:q0 + 128],
                                       pC3[:, pslot + h01, q0:q0 + 128], triB[:]),
                                   reads=[("pC", pslot + h01), "triB"],
                                   writes=[("pC", pslot + h01)], tag="mask")

                def emit_pv(i):
                    a = i % 2
                    q0 = max(0, 128 * i - 512 * j)
                    pslot = 2 + 2 * a
                    for h01 in range(2):
                        hh = 2 * p + h01
                        st_ = (i == 0)
                        sp_ = (i == imax)
                        sc.add("tensor", lambda e, i=i, hh=hh, h01=h01, q0=q0,
                               pslot=pslot, st_=st_, sp_=sp_, pvb=pvb: MM(
                                   e, ps[0:65, pvb + h01, q0:512], Vv[:, i, hh, 0:65],
                                   pC3[:, pslot + h01, q0:512], st_, sp_),
                               reads=[("pC", pslot + h01)] + vregs +
                                     ([pvbank[h01]] if st_ else []),
                               writes=[pvbank[h01]] if sp_ else [],
                               tag="pvmm")

                emit_S(p, j, 0, qslot)
                if imax >= 1:
                    emit_S(p, j, 1, qslot)
                for i in range(imax + 1):
                    emit_exp(i)
                    if i + 2 <= imax:
                        emit_S(p, j, i + 2, qslot)
                    emit_pv(i)
                    for fn in hooks.get(i, ()):
                        fn()

            def znorm_rows(p, j):
                pvb = 4 + 2 * (p % 2)
                pvbank = [("bank", pvb), ("bank", pvb + 1)]
                sl = p % 2
                sc.add("scalar", lambda e, pvb=pvb: e.activation(
                    l64[64:65, 0:1024].rearrange("p (a n) -> p a n", a=2),
                    ps[64:65, pvb:pvb + 2, :], AF.Ln),
                       reads=[pvbank[0], pvbank[1]], writes=["l64"], tag="lln")
                sc.add("scalar", lambda e: e.activation(
                    l64[64:65, 0:1024], l64[64:65, 0:1024], AF.Exp, scale=-1.0),
                       reads=["l64"], writes=["l64"], tag="lexp")
                sc.add("sync", lambda e, sl=sl: e.dma_start(
                    out=dscr[sl:sl + 1, :], in_=l64[64:65, 0:1024]),
                       reads=["l64"], writes=[("scr", sl)], dma=True, tag="lbo")
                sc.add("sync", lambda e, sl=sl: e.dma_start(
                    out=lbsb[:, :], in_=dscr[sl:sl + 1, :].broadcast_to([64, 1024])),
                       reads=[("scr", sl)], writes=["lbsb"], dma=True, tag="lbin")

            def znorm_zmul(p, j):
                pvb = 4 + 2 * (p % 2)
                pvbank = [("bank", pvb), ("bank", pvb + 1)]
                for h01 in range(2):
                    hh = 2 * p + h01
                    zr = ("wB", 4 + hh // 2)
                    sc.add("vector", lambda e, hh=hh, h01=h01, pvb=pvb: e.tensor_mul(
                        zv[0:64, hh, :], ps[0:64, pvb + h01, :],
                        lbsb[:, 512 * h01:512 * (h01 + 1)]),
                           reads=[pvbank[h01], "lbsb"], writes=[zr], tag="zmul")

            def wo_unit(j, d):
                ws = d % 2
                wreg = ("wB", 2 + ws)
                wov = wB[:, 2048 + 1024 * ws:2048 + 1024 * (ws + 1)] \
                    .rearrange("p (h m) -> p h m", h=8)
                sc.add("gpsimd", lambda e, H=H, d=d, wov=wov: e.dma_start(
                    out=wov[0:64, :, :], in_=dwo[H, :, d]),
                       writes=[wreg], dma=True, tag="ldwo")
                bidx = 6 + d % 2
                bank = ("bank", bidx)
                for hh in range(8):
                    zr = ("wB", 4 + hh // 2)
                    sc.add("tensor", lambda e, hh=hh, wov=wov, bidx=bidx: MM(
                        e, ps[:, bidx, :], wov[0:64, hh, :], zv[0:64, hh, :],
                        hh == 0, hh == 7),
                           reads=[wreg, zr],
                           writes=[bank] if hh in (0, 7) else [], tag="womm")
                if H == 0:
                    sc.add("vector", lambda e, d=d, j=j, bidx=bidx:
                           e.scalar_tensor_tensor(
                               xS[:, d, 512 * j:512 * (j + 1)], ps[:, bidx, :],
                               boC[:, d:d + 1], xS[:, d, 512 * j:512 * (j + 1)],
                               ALU.add, ALU.add),
                           reads=[bank, "boC", ("x", d, j)], writes=[("x", d, j)],
                           tag="woevac")
                else:
                    sc.add("vector", lambda e, d=d, j=j, bidx=bidx: e.tensor_add(
                        xS[:, d, 512 * j:512 * (j + 1)],
                        xS[:, d, 512 * j:512 * (j + 1)], ps[:, bidx, :]),
                           reads=[bank, ("x", d, j)], writes=[("x", d, j)],
                           tag="woevac")

            wo_pend = []
            for j in range(4):
                q_bounds(0, j)
                q_bounds2(0, j)
                q_bounds(1, j)
                q_bounds2(1, j)
                pend_rows = None
                pend_zmul = None
                for p in range(4):
                    hooks = {}
                    if pend_rows:
                        hooks.setdefault(0, []).append(pend_rows)
                        pend_rows = None
                    if pend_zmul:
                        hooks.setdefault(1, []).append(pend_zmul)
                        pend_zmul = None
                    if p + 2 < 4:
                        hooks.setdefault(2, []).append(
                            lambda p2=p + 2, j=j: q_bounds(p2, j))
                        hooks.setdefault(3, []).append(
                            lambda p2=p + 2, j=j: q_bounds2(p2, j))
                    if p == 0 and wo_pend:
                        imax = 4 * j + 3
                        for di, fn in enumerate(wo_pend):
                            hooks.setdefault(min(1 + di, imax), []).append(fn)
                        wo_pend = []
                    s_iloop(p, j, hooks)
                    pend_rows = (lambda p0=p, j=j: znorm_rows(p0, j))
                    pend_zmul = (lambda p0=p, j=j: znorm_zmul(p0, j))
                pend_rows()
                pend_zmul()
                wo_pend = [(lambda j0=j, d0=d: wo_unit(j0, d0)) for d in range(8)]
            for fn in wo_pend:
                fn()
            wo_pend = []

    init_consts()
    init_loads()
    ffn_weight_load(dwkF, wAf[:, :, :], WA_ALL, "ldwk1")
    ffn_weight_load(dwrF, wBf[:, :, :], WB_ALL, "ldwr1")
    ln("plain", "ln1")
    ffn("f1")
    if stage in ("attn", "full"):
        ln("attn", "ln2")
        attention()
    if stage == "full":
        ffn_weight_load(dwrF, wBf[:, :, :],
                        WB_ALL + [("wBq", q) for q in range(4)], "ldwr2")
        ln("plain", "ln3")
        ffn("f2")
    for c in range(8):
        sc.add("sync", lambda e, c=c: e.dma_start(out=dout[:, c, :], in_=xS[:, c, :]),
               reads=[("x", c, t) for t in range(4)], writes=[("outd", c)],
               dma=True, tag="store")
    sc.add("sync", lambda e: e.nop(), reads=[("outd", c) for c in range(8)], tag="final")

    sc.finalize()
    semctx = []
    sems = {}
    for cs in ("sync", "tensor", "scalar", "vector", "gpsimd",
               "q_sync", "q_tensor", "q_scalar", "q_vector", "q_gpsimd"):
        g = nc.semaphore("sem_" + cs)
        sems[cs] = g.__enter__()
        semctx.append(g)
    blk = nc.Block()
    block = blk.__enter__()
    block.sync(lambda e: sc.emit_engine("sync", e, sems))
    block.tensor(lambda e: sc.emit_engine("tensor", e, sems))
    block.scalar(lambda e: sc.emit_engine("scalar", e, sems))
    block.vector(lambda e: sc.emit_engine("vector", e, sems))
    block.gpsimd(lambda e: sc.emit_engine("gpsimd", e, sems))
    blk.__exit__(None, None, None)
    for g in reversed(semctx):
        g.__exit__(None, None, None)
    for g in reversed(ctx):
        g.__exit__(None, None, None)
    return nc


def prep_core_inputs(xi, w):
    d = {"x": np.ascontiguousarray(xi.T.reshape(8, 128, 2048).transpose(1, 0, 2))}
    d.update(w)
    return d


def prep_weights(inp):
    f32 = np.float32

    def featw(wT):
        return np.ascontiguousarray(wT.reshape(8, 128, 1024).transpose(1, 0, 2))

    def colz(v):
        return np.ascontiguousarray(np.asarray(v, f32).reshape(8, 128).T)

    w = {}
    w["wrF"] = featw(np.ascontiguousarray(np.asarray(inp["L0_ffn_W_r"], f32).T))
    w["wkF"] = featw(np.ascontiguousarray(np.asarray(inp["L0_ffn_W_k"], f32).T))
    w["wvF"] = featw(np.ascontiguousarray(np.asarray(inp["L0_ffn_W_v"], f32).T))
    wq2 = np.asarray(inp["L0_attn_W_q"], f32).transpose(1, 0, 2).reshape(1024, 1024)
    wk2 = np.asarray(inp["L0_attn_W_k"], f32).transpose(1, 0, 2).reshape(1024, 1024)
    wv2 = np.asarray(inp["L0_attn_W_v"], f32).transpose(1, 0, 2).reshape(1024, 1024)
    w["wq"] = np.ascontiguousarray(
        wq2.reshape(8, 128, 2, 4, 128).transpose(2, 1, 0, 3, 4))
    w["wkA"] = np.ascontiguousarray(wk2.reshape(8, 128, 2, 512).transpose(2, 1, 0, 3))
    w["wvA"] = np.ascontiguousarray(wv2.reshape(8, 128, 2, 512).transpose(2, 1, 0, 3))
    wo = np.asarray(inp["L0_attn_W_o"], f32).reshape(2, 8, 64, 8, 128)
    w["wo"] = np.ascontiguousarray(wo.transpose(0, 2, 3, 1, 4))
    w["mur"] = colz(inp["L0_ffn_mu_r"])
    w["muk"] = colz(inp["L0_ffn_mu_k"])
    w["bq"] = colz(np.asarray(inp["L0_attn_b_q"], f32).reshape(1024))
    w["bk"] = colz(np.asarray(inp["L0_attn_b_k"], f32).reshape(1024))
    w["bo"] = colz(inp["L0_attn_b_o"])
    w["lnw"] = colz(inp["L0_attn_ln_w"])
    w["lnb"] = colz(inp["L0_attn_ln_b"])
    w["bv"] = np.ascontiguousarray(np.asarray(inp["L0_attn_b_v"], f32).reshape(1, 1024))
    w["tri"] = np.ascontiguousarray(np.triu(np.ones((128, 128), np.float32)))
    return w


def kernel(**inputs):
    stage = inputs.pop("_stage", "full")
    nc = build(stage)
    w = prep_weights(inputs)
    x = np.asarray(inputs["L0_attn_resid_stream"], np.float32)
    in_maps = [prep_core_inputs(x[i], w) for i in range(8)]
    res = run_bass_kernel_spmd(nc, in_maps, core_ids=list(range(8)))
    outs = []
    for i in range(8):
        o = res.results[i]["out"]
        outs.append(o.transpose(1, 0, 2).reshape(1024, 2048).T)
    return np.stack(outs, axis=0).astype(np.float32)


# version 12
# speedup vs baseline: 1.0109x; 1.0109x over previous
import numpy as np
import concourse.bass as bass
import concourse.mybir as mybir
from concourse.bass_utils import run_bass_kernel_spmd

F32 = mybir.dt.float32
BF16 = mybir.dt.bfloat16
AF = mybir.ActivationFunctionType
ALU = mybir.AluOpType
AX = mybir.AxisListType.X

ENGS = ("sync", "tensor", "scalar", "vector", "gpsimd")

SQRT_HALF = 0.70710678118654752


class Sched:

    def __init__(self):
        self.by_eng = {e: [] for e in ENGS}
        self.ccount = {}
        self.ops = []
        self.res = {}

    def add(self, eng, fn, reads=(), writes=(), dma=False, tag=""):
        cs = ("q_" + eng) if dma else eng
        cidx = self.ccount.get(cs, 0)
        self.ccount[cs] = cidx + 1
        op = {"eng": eng, "fn": fn, "dma": dma, "cs": cs, "cidx": cidx,
              "deps": {}, "id": len(self.ops), "tag": tag}

        def isbank(r):
            return isinstance(r, tuple) and r and r[0] == "bank"

        def dep(d, bank):
            op["deps"][d] = op["deps"].get(d, False) or bank

        for r in reads:
            e = self.res.get(r)
            if e and e["w"] is not None:
                dep(e["w"], isbank(r))
        for w in writes:
            e = self.res.get(w)
            if e:
                if e["w"] is not None:
                    dep(e["w"], isbank(w))
                for rr in e["r"]:
                    dep(rr, isbank(w))
        self.ops.append(op)
        self.by_eng[eng].append(op)
        for r in reads:
            self.res.setdefault(r, {"w": None, "r": set()})["r"].add(op["id"])
        for w in writes:
            self.res[w] = {"w": op["id"], "r": set()}
        return op["id"]

    @staticmethod
    def _skip(dop, op, bank=False):
        if (not dop["dma"]) and dop["eng"] == op["eng"]:
            if bank and not (dop["eng"] == "tensor" and op["eng"] == "tensor"):
                return False
            return True
        return False

    def finalize(self):
        marks = {}
        for op in self.ops:
            if op["dma"]:
                marks.setdefault(op["cs"], set()).add(op["cidx"])
            for d, bank in op["deps"].items():
                dop = self.ops[d]
                if self._skip(dop, op, bank):
                    continue
                marks.setdefault(dop["cs"], set()).add(dop["cidx"])
        self.pref = {}
        for cs, s in marks.items():
            m = {}
            for i, ci in enumerate(sorted(s)):
                m[ci] = i + 1
            self.pref[cs] = m
        self.marked = set()
        for op in self.ops:
            m = self.pref.get(op["cs"])
            if m and op["cidx"] in m:
                self.marked.add(op["id"])

    def emit_engine(self, ename, eng_obj, sems):
        waited = {}
        for op in self.by_eng[ename]:
            tgts = {}
            for d, bank in op["deps"].items():
                dop = self.ops[d]
                if self._skip(dop, op, bank):
                    continue
                cs = dop["cs"]
                amt = 16 if dop["dma"] else 1
                val = self.pref[cs][dop["cidx"]] * amt
                if tgts.get(cs, 0) < val:
                    tgts[cs] = val
            if op["dma"]:
                qcs = op["cs"]
                want = 16 * op["cidx"]
                if want > 0 and tgts.get(qcs, 0) < want:
                    tgts[qcs] = want
            for cs in sorted(tgts):
                val = tgts[cs]
                if waited.get(cs, 0) < val:
                    eng_obj.wait_ge(sems[cs], val)
                    waited[cs] = val
            inst = op["fn"](eng_obj)
            if op["id"] in self.marked:
                inst.then_inc(sems[op["cs"]], 16 if op["dma"] else 1)


def build(stage="full"):
    nc = bass.Bass("TRN2", target_bir_lowering=False, detect_race_conditions=False)
    sc = Sched()

    dx = nc.dram_tensor("x", [128, 8, 2048], F32, kind="ExternalInput").ap()
    dwrF = nc.dram_tensor("wrF", [128, 8, 1024], F32, kind="ExternalInput").ap()
    dwkF = nc.dram_tensor("wkF", [128, 8, 1024], F32, kind="ExternalInput").ap()
    dwvF = nc.dram_tensor("wvF", [128, 8, 1024], F32, kind="ExternalInput").ap()
    dwq = nc.dram_tensor("wq", [2, 128, 8, 4, 128], F32, kind="ExternalInput").ap()
    dwkA = nc.dram_tensor("wkA", [2, 128, 8, 512], F32, kind="ExternalInput").ap()
    dwvA = nc.dram_tensor("wvA", [2, 128, 8, 512], F32, kind="ExternalInput").ap()
    dwo = nc.dram_tensor("wo", [2, 64, 8, 8, 128], F32, kind="ExternalInput").ap()
    dmur = nc.dram_tensor("mur", [128, 8], F32, kind="ExternalInput").ap()
    dmuk = nc.dram_tensor("muk", [128, 8], F32, kind="ExternalInput").ap()
    dbq = nc.dram_tensor("bq", [128, 8], F32, kind="ExternalInput").ap()
    dbk = nc.dram_tensor("bk", [128, 8], F32, kind="ExternalInput").ap()
    dbo = nc.dram_tensor("bo", [128, 8], F32, kind="ExternalInput").ap()
    dlnw = nc.dram_tensor("lnw", [128, 8], F32, kind="ExternalInput").ap()
    dlnb = nc.dram_tensor("lnb", [128, 8], F32, kind="ExternalInput").ap()
    dbv = nc.dram_tensor("bv", [1, 1024], F32, kind="ExternalInput").ap()
    dtri = nc.dram_tensor("tri", [128, 128], F32, kind="ExternalInput").ap()
    dout = nc.dram_tensor("out", [128, 8, 2048], F32, kind="ExternalOutput").ap()
    dscr = nc.dram_tensor("scr", [5, 1024], F32, kind="Internal").ap()

    ctx = []

    def sb(name, shape, dt):
        g = nc.sbuf_tensor(name, shape, dt)
        t = g.__enter__()
        ctx.append(g)
        return t

    xS = sb("xS", [128, 8, 2048], F32)
    pA = sb("pA", [128, 8, 2048], BF16)
    pB = sb("pB", [128, 16512], BF16)
    pC = sb("pC", [128, 8, 512], BF16)
    wA = sb("wAt", [128, 8, 1024], BF16)
    wB = sb("wBt", [128, 8192], BF16)
    rtmp = sb("rtmp", [128, 2048], F32)
    rowA = sb("rowA", [1, 512], F32)
    rowB = sb("rowB", [1, 512], F32)
    rowC = sb("rowC", [1, 16], F32)
    kns = sb("kns", [1, 16], F32)
    l64 = sb("l64", [65, 1024], F32)
    lbsb = sb("lbsb", [64, 1024], F32)
    nbC = sb("nbC", [128, 4], F32)
    murC = sb("murC", [128, 8], F32)
    mukC = sb("mukC", [128, 8], F32)
    murI = sb("murI", [128, 8], F32)
    mukI = sb("mukI", [128, 8], F32)
    bqC = sb("bqC", [128, 8], F32)
    bkC = sb("bkC", [128, 8], F32)
    boC = sb("boC", [128, 8], F32)
    lnwC = sb("lnwC", [128, 8], F32)
    lnbC = sb("lnbC", [128, 8], F32)
    bvB = sb("bvB", [1, 1024], BF16)
    triB = sb("triB", [128, 128], BF16)
    onesB = sb("onesB", [128, 128], BF16)
    oTop = sb("oTop", [128, 1], BF16)
    oBot = sb("oBot", [128, 1], BF16)
    consts = sb("consts", [128, 12], F32)
    onesFc = sb("onesFc", [128, 1], F32)
    wqh = sb("wqh", [128, 8, 4, 128], BF16)
    psg = nc.psum_tensor("ps", [128, 8, 512], F32)
    ps = psg.__enter__()
    ctx.append(psg)

    pC3 = pC
    pBv = pB[:, 0:16384].rearrange("p (c t) -> p c t", c=8)
    Kv = pB[:, 0:8192].rearrange("p (a t) -> p a t", a=4)
    Vv = pB[:, 8192:16512].rearrange("p (t h e) -> p t h e", t=16, h=8)
    rbufB = rtmp[:].bitcast(BF16).rearrange("p (a n) -> p a n", a=8)
    zv = wB[:, 4096:8192].rearrange("p (h n) -> p h n", h=8)
    wBq = wB[:, 0:2048].rearrange("p (q n) -> p q n", q=4)
    wBf = wB[:].rearrange("p (c d) -> p c d", c=8)
    wAf = wA
    wkhA = wA[:].rearrange("p c (x n) -> p (c x) n", x=2)[:, 0:8, :]
    wvhA = wA[:].rearrange("p c (x n) -> p (c x) n", x=2)[:, 8:16, :]
    wqv = pC[:, 6:8, :].rearrange("p a n -> p (a n)").rearrange("p (c m) -> p c m", c=8)

    def pBr(lo, hi):
        return ["pB%d" % s for s in range(lo // 2048, (hi + 2047) // 2048)]

    K_CONST = ["consts"]
    WA_ALL = [("wA", c) for c in range(8)]
    WB_ALL = [("wB", c) for c in range(8)]

    def MM(e, out, lhsT, rhs, start, stop, tp=None):
        return nc.tensor.matmul(out, lhsT, rhs, start=start, stop=stop,
                                tile_position=tp, skip_group_check=True)

    CONSTVALS = [(-1.0, 0), (0.5, 1), (SQRT_HALF, 2), (1.0 / 1024.0, 3), (0.125, 4),
                 (-0.5, 5), (1e-5, 6), (-0.05, 7), (-1.02, 8), (2.0, 9), (-1e30, 10)]

    def init_consts():
        def f(v, col):
            return lambda e, v=v, col=col: e.memset(consts[:, col:col + 1], v)
        for v, col in CONSTVALS:
            sc.add("vector", f(v, col), writes=K_CONST)
            nc.const_aps.register(F32, v, consts[:, col:col + 1])
        sc.add("vector", lambda e: e.memset(onesB[:], 1.0), writes=K_CONST)
        sc.add("vector", lambda e: e.memset(onesFc[:], 1.0), writes=K_CONST)
        sc.add("vector", lambda e: e.memset(oTop[0:64, :], 1.0), writes=K_CONST)
        sc.add("vector", lambda e: e.memset(oTop[64:128, :], 0.0), writes=K_CONST)
        sc.add("vector", lambda e: e.memset(oBot[0:64, :], 0.0), writes=K_CONST)
        sc.add("vector", lambda e: e.memset(oBot[64:128, :], 1.0), writes=K_CONST)
        for en in ("tensor", "scalar", "gpsimd", "sync"):
            sc.add(en, lambda e: e.nop(), reads=K_CONST)

    def init_loads():
        sc.add("sync", lambda e: e.dma_start(out=xS[:, 0:4, :], in_=dx[:, 0:4, :]),
               writes=[("x", c, t) for c in range(4) for t in range(4)],
               dma=True, tag="ldx")
        sc.add("sync", lambda e: e.dma_start(out=xS[:, 4:8, :], in_=dx[:, 4:8, :]),
               writes=[("x", c, t) for c in range(4, 8) for t in range(4)],
               dma=True, tag="ldx")
        for dst, src, nm in ((murC, dmur, "murC"), (mukC, dmuk, "mukC"),
                             (bqC, dbq, "bqC"), (bkC, dbk, "bkC"), (boC, dbo, "boC"),
                             (lnwC, dlnw, "lnwC"), (lnbC, dlnb, "lnbC")):
            sc.add("sync", lambda e, d=dst, s=src: e.dma_start(out=d[:], in_=s[:]),
                   writes=[nm], dma=True)
        sc.add("gpsimd", lambda e: e.dma_start(out=bvB[:], in_=dbv[:]), writes=["bvB"], dma=True)
        sc.add("gpsimd", lambda e: e.dma_start(out=triB[:], in_=dtri[:]), writes=["triB"], dma=True)
        sc.add("scalar", lambda e: e.mul(bqC[:], bqC[:], 0.125),
               reads=K_CONST + ["bqC"], writes=["bqC"])
        sc.add("vector", lambda e: e.tensor_scalar(murI[:], murC[:], -1.0, 1.0, ALU.mult, ALU.add),
               reads=K_CONST + ["murC"], writes=["murI"])
        sc.add("vector", lambda e: e.tensor_scalar(mukI[:], mukC[:], -1.0, 1.0, ALU.mult, ALU.add),
               reads=K_CONST + ["mukC"], writes=["mukI"])

    def ln(kind, tag):
        apply_pend = []
        for t in range(4):
            tsl0 = slice(512 * t, 512 * (t + 1))
            for c in range(8):
                reg = pBr(2048 * c, 2048 * (c + 1))
                sc.add("tensor",
                       lambda e, c=c, t=t, tsl0=tsl0: MM(e, ps[0:1, t, :], onesFc[:, 0:1],
                                                         xS[:, c, tsl0],
                                                         c == 0, c == 7),
                       reads=[("x", c, t)],
                       writes=[("bank", t)] if c in (0, 7) else [],
                       tag=tag + "s1")
                sc.add("vector", lambda e, c=c, tsl0=tsl0: e.tensor_mul(
                    pBv[:, c, tsl0], xS[:, c, tsl0], xS[:, c, tsl0]),
                       reads=[("x", c, t)] + reg, writes=reg, tag=tag + "sq")
                sc.add("tensor",
                       lambda e, c=c, t=t, tsl0=tsl0: MM(e, ps[0:1, 4 + t, :], onesB[:, 0:1],
                                                         pBv[:, c, tsl0],
                                                         c == 0, c == 7),
                       reads=reg,
                       writes=[("bank", 4 + t)] if c in (0, 7) else [],
                       tag=tag + "s2")
            s1 = ps[0:1, t, :]
            s2 = ps[0:1, 4 + t, :]
            bt, b4t = ("bank", t), ("bank", 4 + t)
            sc.add("scalar", lambda e, s1=s1: e.mul(s1, s1, 1.0 / 1024.0),
                   reads=[bt] + K_CONST, writes=[bt], tag=tag + "mean")
            sc.add("scalar", lambda e, s2=s2: e.mul(s2, s2, 1.0 / 1024.0),
                   reads=[b4t] + K_CONST, writes=[b4t], tag=tag + "msq")
            sc.add("scalar", lambda e, s1=s1: e.activation(rowA[:], s1, AF.Square),
                   reads=[bt], writes=["rowA"])
            sc.add("vector", lambda e, s2=s2: e.tensor_sub(rowA[:], s2, rowA[:]),
                   reads=[b4t, "rowA"], writes=["rowA"])
            if kind == "plain":
                sc.add("scalar", lambda e: e.activation(rowA[:], rowA[:], AF.Ln,
                                                        bias=consts[0:1, 6:7]),
                       reads=["rowA"] + K_CONST, writes=["rowA"])
                sc.add("scalar", lambda e: e.activation(rowA[:], rowA[:], AF.Exp, scale=-0.5),
                       reads=["rowA"], writes=["rowA"])
            else:
                sc.add("scalar", lambda e: e.activation(rowB[:], rowA[:], AF.Ln,
                                                        bias=consts[0:1, 6:7]),
                       reads=["rowA"] + K_CONST, writes=["rowB"])
                sc.add("scalar", lambda e: e.activation(l64[0:1, 0:512], rowB[:], AF.Exp,
                                                        scale=-1.0),
                       reads=["rowB"], writes=["l64"])
                sc.add("vector", lambda e: e.tensor_mul(l64[0:1, 0:512], rowA[:],
                                                        l64[0:1, 0:512]),
                       reads=["rowA", "l64"], writes=["l64"])
                sc.add("scalar", lambda e: e.activation(l64[0:1, 0:512], l64[0:1, 0:512],
                                                        AF.Ln, bias=consts[0:1, 6:7]),
                       reads=["l64"] + K_CONST, writes=["l64"])
                sc.add("vector", lambda e: e.tensor_add(rowB[:], rowB[:], l64[0:1, 0:512]),
                       reads=["rowB", "l64"], writes=["rowB"])
                sc.add("scalar", lambda e: e.activation(rowA[:], rowB[:], AF.Exp, scale=-0.5),
                       reads=["rowB"], writes=["rowA"])
            sc.add("vector", lambda e, s1=s1: e.scalar_tensor_tensor(
                rowB[:], s1, -1.0, rowA[:], ALU.mult, ALU.mult),
                   reads=[bt, "rowA"] + K_CONST, writes=["rowB"])
            half = 1024 * (t % 2)
            scr_sl = 2 if t % 2 == 0 else 4
            rt_names = ["rtQ0", "rtQ1"] if t % 2 == 0 else ["rtQ2", "rtQ3"]
            sc.add("sync", lambda e, scr_sl=scr_sl: e.dma_start(
                out=dscr[scr_sl:scr_sl + 1, 0:512], in_=rowA[:]),
                   reads=["rowA"], writes=[("scr", scr_sl)], dma=True, tag=tag + "bcAo")
            sc.add("sync", lambda e, scr_sl=scr_sl: e.dma_start(
                out=dscr[scr_sl:scr_sl + 1, 512:1024], in_=rowB[:]),
                   reads=["rowB"], writes=[("scr", scr_sl)], dma=True, tag=tag + "bcBo")
            sc.add("sync", lambda e, half=half, scr_sl=scr_sl: e.dma_start(
                out=rtmp[:, half:half + 1024],
                in_=dscr[scr_sl:scr_sl + 1, :].broadcast_to([128, 1024])),
                   reads=[("scr", scr_sl)], writes=rt_names, dma=True, tag=tag + "bcin")
            def mk_apply(t, half, rt_names):
                def emit():
                    for c in range(8):
                        tsl = slice(512 * t, 512 * (t + 1))
                        sc.add("vector", lambda e, c=c, tsl=tsl, half=half: e.tensor_mul(
                            pC3[:, c, :], xS[:, c, tsl], rtmp[:, half:half + 512]),
                               reads=[("x", c, t), rt_names[0]], writes=[("pC", c)],
                               tag=tag + "ap1")
                        if kind == "plain":
                            sc.add("vector", lambda e, c=c, tsl=tsl, half=half:
                                   e.tensor_add(
                                       pA[:, c, tsl], pC3[:, c, :],
                                       rtmp[:, half + 512:half + 1024]),
                                   reads=[("pC", c), rt_names[1]],
                                   writes=[("pA", c, t)], tag=tag + "ap2")
                        else:
                            sc.add("vector", lambda e, c=c, half=half: e.tensor_add(
                                pC3[:, c, :], pC3[:, c, :],
                                rtmp[:, half + 512:half + 1024]),
                                   reads=[("pC", c), rt_names[1]], writes=[("pC", c)],
                                   tag=tag + "ap2")
                            sc.add("vector", lambda e, c=c, tsl=tsl: e.tensor_scalar(
                                pA[:, c, tsl], pC3[:, c, :], lnwC[:, c:c + 1],
                                lnbC[:, c:c + 1], ALU.mult, ALU.add),
                                   reads=[("pC", c), "lnwC", "lnbC"],
                                   writes=[("pA", c, t)], tag=tag + "ap3")
                return emit
            apply_pend.append(mk_apply(t, half, rt_names))
            if len(apply_pend) > 1:
                apply_pend.pop(0)()
        for fn in apply_pend:
            fn()
        apply_pend = []
        if True:
            pass

    def ffn_weight_load(src, dst_view, dst_regs, tag):
        sc.add("gpsimd", lambda e, s=src, d=dst_view: e.dma_start(out=d, in_=s[:]),
               writes=dst_regs, dma=True, tag=tag)

    def mix(tag, t, mu, muI, muname, muIname):
        lo = 512 * t
        for c in range(8):
            if t == 0:
                sc.add("vector", lambda e, c=c: e.memset(pC3[:, c, 0:1], 0.0),
                       writes=[("pC", c)], tag=tag)
                sc.add("vector", lambda e, c=c, muI=muI: e.tensor_scalar(
                    pC3[:, c, 1:512], pA[:, c, 0:511], muI[:, c:c + 1], None, ALU.mult),
                       reads=[("pA", c, 0), muIname], writes=[("pC", c)], tag=tag)
            else:
                sc.add("vector", lambda e, c=c, lo=lo, muI=muI: e.tensor_scalar(
                    pC3[:, c, :], pA[:, c, lo - 1:lo + 511], muI[:, c:c + 1], None, ALU.mult),
                       reads=[("pA", c, t), ("pA", c, t - 1), muIname],
                       writes=[("pC", c)], tag=tag)
            sc.add("vector", lambda e, c=c, lo=lo, mu=mu: e.scalar_tensor_tensor(
                pC3[:, c, :], pA[:, c, lo:lo + 512], mu[:, c:c + 1], pC3[:, c, :],
                ALU.mult, ALU.add),
                   reads=[("pA", c, t), ("pC", c), muname], writes=[("pC", c)], tag=tag)

    def ffn(tag):
        kact = [pBr(2048 * d, 2048 * (d + 1)) for d in range(8)]
        for t in range(4):
            mix(tag + "mxk", t, mukC, mukI, "mukC", "mukI")
            for d in range(8):
                bidx = d % 4
                bank = ("bank", bidx)
                for c in range(8):
                    sc.add("tensor", lambda e, c=c, d=d, bidx=bidx: MM(
                        e, ps[:, bidx, :], wAf[:, c, 128 * d:128 * (d + 1)], pC3[:, c, :],
                        c == 0, c == 7),
                           reads=[("wA", c), ("pC", c)],
                           writes=[bank] if c in (0, 7) else [], tag=tag + "kmm")
                rh = slice(0, 512) if d % 2 == 0 else slice(512, 1024)
                rtr = ["rtQ0", "rtQ1"] if d % 2 == 0 else ["rtQ2", "rtQ3"]
                sc.add("scalar", lambda e, bidx=bidx, rh=rh: e.activation(
                    rtmp[:, rh], ps[:, bidx, :], AF.Relu),
                       reads=[bank], writes=rtr, tag=tag + "relu")
                sc.add("scalar", lambda e, d=d, t=t, rh=rh: e.activation(
                    pBv[:, d, 512 * t:512 * (t + 1)], rtmp[:, rh], AF.Square,
                    scale=SQRT_HALF),
                       reads=rtr, writes=kact[d], tag=tag + "sq")
        ffn_weight_load(dwvF, wAf[:, :, :], WA_ALL, tag + "ldwv")
        def emit_r(t, d):
            bidx = 2 + d % 2
            bank = ("bank", bidx)
            for c in range(8):
                sc.add("tensor", lambda e, c=c, d=d, bidx=bidx: MM(
                    e, ps[:, bidx, :], wBf[:, c, 128 * d:128 * (d + 1)], pC3[:, c, :],
                    c == 0, c == 7),
                       reads=[("wB", c), ("pC", c)],
                       writes=[bank] if c in (0, 7) else [], tag=tag + "rmm")
            rb = "rtQ%d" % (d % 2)
            sc.add("scalar", lambda e, bidx=bidx, d=d: e.activation(
                rbufB[:, d % 2, :], ps[:, bidx, :], AF.Sigmoid),
                   reads=[bank], writes=[rb], tag=tag + "sig")

        def emit_v(t, d):
            bidx = 4 + d % 2
            bank = ("bank", bidx)
            for c in range(8):
                sc.add("tensor", lambda e, c=c, d=d, t=t, bidx=bidx: MM(
                    e, ps[:, bidx, :], wAf[:, c, 128 * d:128 * (d + 1)],
                    pBv[:, c, 512 * t:512 * (t + 1)], c == 0, c == 7),
                       reads=[("wA", c)] + kact[c],
                       writes=[bank] if c in (0, 7) else [], tag=tag + "vmm")
            rb = "rtQ%d" % (d % 2)
            tb = "rtQ%d" % (2 + d % 2)
            sc.add("vector", lambda e, bidx=bidx, d=d: e.tensor_mul(
                rbufB[:, 2 + d % 2, :], ps[:, bidx, :], rbufB[:, d % 2, :]),
                   reads=[bank, rb], writes=[tb], tag=tag + "vmul")
            sc.add("vector", lambda e, d=d, t=t: e.tensor_add(
                xS[:, d, 512 * t:512 * (t + 1)], xS[:, d, 512 * t:512 * (t + 1)],
                rbufB[:, 2 + d % 2, :]),
                   reads=[tb, ("x", d, t)], writes=[("x", d, t)], tag=tag + "xadd")

        for t in range(4):
            mix(tag + "mxr", t, murC, murI, "murC", "murI")
            for d in range(8):
                emit_r(t, d)
                if d > 0:
                    emit_v(t, d - 1)
            emit_v(t, 7)

    def attention():
        for H in range(2):
            vregs = pBr(8192, 16512)
            kregs = [pBr(2048 * p, 2048 * (p + 1)) for p in range(4)]
            wk_regs = [("wA", s) for s in range(4)]
            wv_regs = [("wA", 4 + s) for s in range(4)]
            sc.add("gpsimd", lambda e, H=H: e.dma_start(out=wkhA[:, :, :], in_=dwkA[H]),
                   writes=wk_regs, dma=True, tag="ldwkA")
            sc.add("gpsimd", lambda e, H=H: e.dma_start(out=wvhA[:, :, :], in_=dwvA[H]),
                   writes=wv_regs, dma=True, tag="ldwvA")
            sc.add("gpsimd", lambda e, H=H: e.dma_start(out=wqh[:, :, :, :], in_=dwq[H]),
                   writes=[("wqh", p) for p in range(4)], dma=True, tag="ldwqh")
            sc.add("vector", lambda e: e.memset(Vv[:, :, :, 64:65], 1.0),
                   writes=vregs, tag="vones")
            for tt in range(16):
                bidx = tt % 4
                bank = ("bank", bidx)
                for c in range(8):
                    sc.add("tensor", lambda e, c=c, tt=tt, bidx=bidx: MM(
                        e, ps[:, bidx, :], pA[:, c, 128 * tt:128 * (tt + 1)], wvhA[:, c, :],
                        c == 0, False),
                           reads=[("pA", c, tt // 4), ("wA", 4 + c // 2)],
                           writes=[bank] if c == 0 else [], tag="vprojmm")
                sc.add("tensor", lambda e, H=H, bidx=bidx: MM(
                    e, ps[:, bidx, :], onesB[0:1, :], bvB[0:1, 512 * H:512 * (H + 1)],
                    False, True),
                       reads=["bvB"], writes=[bank], tag="vbias")
                sc.add("scalar", lambda e, tt=tt, bidx=bidx: e.copy(
                    Vv[:, tt, :, 0:64], ps[:, bidx, :].rearrange("p (h e) -> p h e", h=8)),
                       reads=[bank], writes=vregs, tag="vevac")
            sc.add("vector", lambda e: e.memset(kns[:, :], 0.0), writes=["kns"], tag="kns0")
            kn_pend = []
            for p in range(4):
                for t in range(4):
                    bidx = (p * 4 + t) % 4
                    bank = ("bank", bidx)
                    for c in range(8):
                        sc.add("tensor", lambda e, c=c, p=p, t=t, bidx=bidx: MM(
                            e, ps[:, bidx, :], wkhA[:, c, 128 * p:128 * (p + 1)],
                            pA[:, c, 512 * t:512 * (t + 1)], c == 0, c == 7),
                               reads=[("pA", c, t), ("wA", c // 2)],
                               writes=[bank] if c in (0, 7) else [], tag="kprojmm")
                    for fn in kn_pend:
                        fn()
                    kn_pend = []
                    sc.add("vector", lambda e, p=p, t=t, H=H, bidx=bidx: e.tensor_scalar(
                        Kv[:, p, 512 * t:512 * (t + 1)], ps[:, bidx, :],
                        bkC[:, 4 * H + p:4 * H + p + 1], None, ALU.add),
                           reads=[bank, "bkC"], writes=kregs[p], tag="kevac")
                    sc.add("vector", lambda e, p=p, t=t: e.tensor_mul(
                        rbufB[:, 0, :], Kv[:, p, 512 * t:512 * (t + 1)],
                        Kv[:, p, 512 * t:512 * (t + 1)]),
                           reads=kregs[p], writes=["rtQ0"], tag="k2")

                    def mk_kn(p, H):
                        def emit():
                            for h01 in range(2):
                                col = 2 * (4 * H + p) + h01
                                onesc = oTop if h01 == 0 else oBot
                                bnk = ("bank", 6 + h01)
                                sc.add("tensor", lambda e, h01=h01, onesc=onesc: MM(
                                    e, ps[0:1, 6 + h01, :], onesc[:, 0:1], rbufB[:, 0, :],
                                    True, True),
                                       reads=["rtQ0"] + K_CONST, writes=[bnk], tag="knmm")
                                sc.add("vector", lambda e, h01=h01, col=col: e.reduce_max(
                                    rowC[0:1, 3 + h01:4 + h01], ps[0:1, 6 + h01, :],
                                    axis=AX),
                                       reads=[bnk], writes=["rowC"], tag="knred")
                                sc.add("vector", lambda e, col=col, h01=h01: e.tensor_max(
                                    kns[0:1, col:col + 1], kns[0:1, col:col + 1],
                                    rowC[0:1, 3 + h01:4 + h01]),
                                       reads=["rowC", "kns"], writes=["kns"], tag="knmax")
                        return emit
                    kn_pend = [mk_kn(p, H)]
            for fn in kn_pend:
                fn()
            lo, hi = 8 * H, 8 * H + 8
            sc.add("scalar", lambda e, lo=lo, hi=hi: e.activation(
                kns[0:1, lo:hi], kns[0:1, lo:hi], AF.Ln, bias=consts[0:1, 6:7]),
                   reads=["kns"] + K_CONST, writes=["kns"], tag="knsln")
            sc.add("scalar", lambda e, lo=lo, hi=hi: e.activation(
                kns[0:1, lo:hi], kns[0:1, lo:hi], AF.Exp, scale=0.5),
                   reads=["kns"], writes=["kns"], tag="knsexp")

            if H == 1 and stage == "full":
                ffn_weight_load(dwkF, wAf[:, :, :], WA_ALL, "pf_wk2")

            def q_bounds(p, j):
                qslot = p
                qb = 4 + 2 * ((p + 1) % 2)
                bank6 = ("bank", qb)
                bank7 = ("bank", qb + 1)
                for c in range(8):
                    sc.add("tensor", lambda e, c=c, j=j, qb=qb, p=p: MM(
                        e, ps[:, qb, :], wqh[:, c, p, :], pA[:, c, 512 * j:512 * (j + 1)],
                        c == 0, c == 7),
                           reads=[("pA", c, j), ("wqh", p)],
                           writes=[bank6] if c in (0, 7) else [], tag="qmm")

            def q_bounds2(p, j):
                qslot = p
                qb = 4 + 2 * ((p + 1) % 2)
                bank6 = ("bank", qb)
                bank7 = ("bank", qb + 1)
                sc.add("vector", lambda e, p=p, H=H, qslot=qslot, qb=qb: e.tensor_scalar(
                    wBq[:, qslot, :], ps[:, qb, :], 0.125,
                    bqC[:, 4 * H + p:4 * H + p + 1], ALU.mult, ALU.add),
                       reads=[bank6, "bqC"] + K_CONST, writes=[("wBq", qslot)], tag="qevac")
                sc.add("vector", lambda e, qslot=qslot: e.tensor_mul(
                    rbufB[:, 4, :], wBq[:, qslot, :], wBq[:, qslot, :]),
                       reads=[("wBq", qslot)], writes=["rtQ2"], tag="q2")
                for h01 in range(2):
                    onesc = oTop if h01 == 0 else oBot
                    bnk = ("bank", qb + h01)
                    sc.add("tensor", lambda e, h01=h01, onesc=onesc, qb=qb: MM(
                        e, ps[0:1, qb + h01, :], onesc[:, 0:1], rbufB[:, 4, :],
                        True, True),
                           reads=["rtQ2"] + K_CONST,
                           writes=[bank6 if h01 == 0 else bank7], tag="qnmm")
                    sc.add("vector", lambda e, h01=h01, qb=qb: e.reduce_max(
                        rowC[0:1, h01:h01 + 1], ps[0:1, qb + h01, :], axis=AX),
                           reads=[("bank", qb + h01)], writes=["rowC"], tag="qnred")
                sc.add("scalar", lambda e: e.activation(rowC[0:1, 0:2], rowC[0:1, 0:2],
                                                        AF.Ln, bias=consts[0:1, 6:7]),
                       reads=["rowC"] + K_CONST, writes=["rowC"], tag="bln")
                sc.add("scalar", lambda e: e.activation(rowC[0:1, 0:2], rowC[0:1, 0:2],
                                                        AF.Exp, scale=0.5),
                       reads=["rowC"], writes=["rowC"], tag="bexp")
                hcols = 2 * (4 * H + p)
                sc.add("vector", lambda e, hcols=hcols: e.tensor_mul(
                    rowC[0:1, 0:2], rowC[0:1, 0:2], kns[0:1, hcols:hcols + 2]),
                       reads=["rowC", "kns"], writes=["rowC"], tag="bmul")
                sc.add("vector", lambda e: e.tensor_scalar(
                    rowC[0:1, 0:2], rowC[0:1, 0:2], -1.02, -0.05, ALU.mult, ALU.add),
                       reads=["rowC"] + K_CONST, writes=["rowC"], tag="bmargin")
                sc.add("vector", lambda e: e.tensor_reduce(
                    rowC[0:1, 2:3], rowC[0:1, 0:2], AX, ALU.min),
                       reads=["rowC"], writes=["rowC"], tag="bmin")
                sc.add("sync", lambda e, qslot=qslot: e.dma_start(
                    out=dscr[3:4, qslot:qslot + 1], in_=rowC[0:1, 2:3]),
                       reads=["rowC"], writes=[("scr3", qslot)], dma=True, tag="nbo")
                sc.add("sync", lambda e, qslot=qslot: e.dma_start(
                    out=nbC[:, qslot:qslot + 1],
                    in_=dscr[3:4, qslot:qslot + 1].broadcast_to([128, 1])),
                       reads=[("scr3", qslot)], writes=[("nb", qslot)], dma=True, tag="nbin")

            def emit_S(p, j, i, qslot):
                a = i % 2
                q0 = max(0, 128 * i - 512 * j)
                for h01 in range(2):
                    sbk = ("bank", 2 * a + h01)
                    sc.add("tensor", lambda e, p=p, i=i, h01=h01, q0=q0, a=a,
                           qslot=qslot: MM(
                               e, ps[:, 2 * a + h01, q0:512],
                               Kv[64 * h01:64 * h01 + 64, p, 128 * i:128 * (i + 1)],
                               wBq[64 * h01:64 * h01 + 64, qslot, q0:512],
                               True, True, tp=(64 * h01, 0)),
                           reads=[("wBq", qslot)] + kregs[p],
                           writes=[sbk], tag="smm")

            def s_iloop(p, j, hooks):
                qslot = p
                pvb = 4 + 2 * (p % 2)
                pvbank = [("bank", pvb), ("bank", pvb + 1)]
                imax = 4 * j + 3

                def emit_exp(i):
                    a = i % 2
                    q0 = max(0, 128 * i - 512 * j)
                    pslot = 2 + 2 * a
                    sc.add("scalar", lambda e, a=a, q0=q0, pslot=pslot, qslot=qslot:
                           e.activation(
                               pC3[:, pslot:pslot + 2, q0:512],
                               ps[:, 2 * a:2 * a + 2, q0:512],
                               AF.Exp, bias=nbC[:, qslot:qslot + 1]),
                           reads=[("bank", 2 * a), ("bank", 2 * a + 1), ("nb", qslot)],
                           writes=[("pC", pslot), ("pC", pslot + 1)], tag="exp")
                    if i >= 4 * j:
                        for h01 in range(2):
                            sc.add("vector", lambda e, pslot=pslot, h01=h01, q0=q0:
                                   e.tensor_mul(
                                       pC3[:, pslot + h01, q0:q0 + 128],
                                       pC3[:, pslot + h01, q0:q0 + 128], triB[:]),
                                   reads=[("pC", pslot + h01), "triB"],
                                   writes=[("pC", pslot + h01)], tag="mask")

                def emit_pv(i):
                    a = i % 2
                    q0 = max(0, 128 * i - 512 * j)
                    pslot = 2 + 2 * a
                    for h01 in range(2):
                        hh = 2 * p + h01
                        st_ = (i == 0)
                        sp_ = (i == imax)
                        sc.add("tensor", lambda e, i=i, hh=hh, h01=h01, q0=q0,
                               pslot=pslot, st_=st_, sp_=sp_, pvb=pvb: MM(
                                   e, ps[0:65, pvb + h01, q0:512], Vv[:, i, hh, 0:65],
                                   pC3[:, pslot + h01, q0:512], st_, sp_),
                               reads=[("pC", pslot + h01)] + vregs +
                                     ([pvbank[h01]] if st_ else []),
                               writes=[pvbank[h01]] if sp_ else [],
                               tag="pvmm")

                emit_S(p, j, 0, qslot)
                if imax >= 1:
                    emit_S(p, j, 1, qslot)
                for i in range(imax + 1):
                    emit_exp(i)
                    if i + 2 <= imax:
                        emit_S(p, j, i + 2, qslot)
                    emit_pv(i)
                    for fn in hooks.get(i, ()):
                        fn()

            def znorm_rows(p, j):
                pvb = 4 + 2 * (p % 2)
                pvbank = [("bank", pvb), ("bank", pvb + 1)]
                sl = p % 2
                sc.add("scalar", lambda e, pvb=pvb: e.activation(
                    l64[64:65, 0:1024].rearrange("p (a n) -> p a n", a=2),
                    ps[64:65, pvb:pvb + 2, :], AF.Ln),
                       reads=[pvbank[0], pvbank[1]], writes=["l64"], tag="lln")
                sc.add("scalar", lambda e: e.activation(
                    l64[64:65, 0:1024], l64[64:65, 0:1024], AF.Exp, scale=-1.0),
                       reads=["l64"], writes=["l64"], tag="lexp")
                sc.add("sync", lambda e, sl=sl: e.dma_start(
                    out=dscr[sl:sl + 1, :], in_=l64[64:65, 0:1024]),
                       reads=["l64"], writes=[("scr", sl)], dma=True, tag="lbo")
                sc.add("sync", lambda e, sl=sl: e.dma_start(
                    out=lbsb[:, :], in_=dscr[sl:sl + 1, :].broadcast_to([64, 1024])),
                       reads=[("scr", sl)], writes=["lbsb"], dma=True, tag="lbin")

            def znorm_zmul(p, j):
                pvb = 4 + 2 * (p % 2)
                pvbank = [("bank", pvb), ("bank", pvb + 1)]
                for h01 in range(2):
                    hh = 2 * p + h01
                    zr = ("wB", 4 + hh // 2)
                    sc.add("vector", lambda e, hh=hh, h01=h01, pvb=pvb: e.tensor_mul(
                        zv[0:64, hh, :], ps[0:64, pvb + h01, :],
                        lbsb[:, 512 * h01:512 * (h01 + 1)]),
                           reads=[pvbank[h01], "lbsb"], writes=[zr], tag="zmul")

            def wo_unit(j, d):
                ws = d % 2
                wreg = ("wB", 2 + ws)
                wov = wB[:, 2048 + 1024 * ws:2048 + 1024 * (ws + 1)] \
                    .rearrange("p (h m) -> p h m", h=8)
                sc.add("gpsimd", lambda e, H=H, d=d, wov=wov: e.dma_start(
                    out=wov[0:64, :, :], in_=dwo[H, :, d]),
                       writes=[wreg], dma=True, tag="ldwo")
                bidx = 6 + d % 2
                bank = ("bank", bidx)
                for hh in range(8):
                    zr = ("wB", 4 + hh // 2)
                    sc.add("tensor", lambda e, hh=hh, wov=wov, bidx=bidx: MM(
                        e, ps[:, bidx, :], wov[0:64, hh, :], zv[0:64, hh, :],
                        hh == 0, hh == 7),
                           reads=[wreg, zr],
                           writes=[bank] if hh in (0, 7) else [], tag="womm")
                if H == 0:
                    sc.add("vector", lambda e, d=d, j=j, bidx=bidx:
                           e.scalar_tensor_tensor(
                               xS[:, d, 512 * j:512 * (j + 1)], ps[:, bidx, :],
                               boC[:, d:d + 1], xS[:, d, 512 * j:512 * (j + 1)],
                               ALU.add, ALU.add),
                           reads=[bank, "boC", ("x", d, j)], writes=[("x", d, j)],
                           tag="woevac")
                else:
                    sc.add("vector", lambda e, d=d, j=j, bidx=bidx: e.tensor_add(
                        xS[:, d, 512 * j:512 * (j + 1)],
                        xS[:, d, 512 * j:512 * (j + 1)], ps[:, bidx, :]),
                           reads=[bank, ("x", d, j)], writes=[("x", d, j)],
                           tag="woevac")

            wo_pend = []
            for j in range(4):
                q_bounds(0, j)
                q_bounds2(0, j)
                q_bounds(1, j)
                q_bounds2(1, j)
                pend_rows = None
                pend_zmul = None
                for p in range(4):
                    hooks = {}
                    if pend_rows:
                        hooks.setdefault(0, []).append(pend_rows)
                        pend_rows = None
                    if pend_zmul:
                        hooks.setdefault(1, []).append(pend_zmul)
                        pend_zmul = None
                    if p + 2 < 4:
                        hooks.setdefault(2, []).append(
                            lambda p2=p + 2, j=j: q_bounds(p2, j))
                        hooks.setdefault(3, []).append(
                            lambda p2=p + 2, j=j: q_bounds2(p2, j))
                    if p == 0 and wo_pend:
                        imax = 4 * j + 3
                        for di, fn in enumerate(wo_pend):
                            hooks.setdefault(min(1 + di, imax), []).append(fn)
                        wo_pend = []
                    s_iloop(p, j, hooks)
                    pend_rows = (lambda p0=p, j=j: znorm_rows(p0, j))
                    pend_zmul = (lambda p0=p, j=j: znorm_zmul(p0, j))
                pend_rows()
                pend_zmul()
                wo_pend = [(lambda j0=j, d0=d: wo_unit(j0, d0)) for d in range(8)]
            for fn in wo_pend:
                fn()
            wo_pend = []

    init_consts()
    init_loads()
    ffn_weight_load(dwkF, wAf[:, :, :], WA_ALL, "ldwk1")
    ffn_weight_load(dwrF, wBf[:, :, :], WB_ALL, "ldwr1")
    ln("plain", "ln1")
    ffn("f1")
    if stage in ("attn", "full"):
        ln("attn", "ln2")
        attention()
    if stage == "full":
        ffn_weight_load(dwrF, wBf[:, :, :],
                        WB_ALL + [("wBq", q) for q in range(4)], "ldwr2")
        ln("plain", "ln3")
        ffn("f2")
    for c in range(8):
        sc.add("sync", lambda e, c=c: e.dma_start(out=dout[:, c, :], in_=xS[:, c, :]),
               reads=[("x", c, t) for t in range(4)], writes=[("outd", c)],
               dma=True, tag="store")
    sc.add("sync", lambda e: e.nop(), reads=[("outd", c) for c in range(8)], tag="final")

    sc.finalize()
    semctx = []
    sems = {}
    for cs in ("sync", "tensor", "scalar", "vector", "gpsimd",
               "q_sync", "q_tensor", "q_scalar", "q_vector", "q_gpsimd"):
        g = nc.semaphore("sem_" + cs)
        sems[cs] = g.__enter__()
        semctx.append(g)
    blk = nc.Block()
    block = blk.__enter__()
    block.sync(lambda e: sc.emit_engine("sync", e, sems))
    block.tensor(lambda e: sc.emit_engine("tensor", e, sems))
    block.scalar(lambda e: sc.emit_engine("scalar", e, sems))
    block.vector(lambda e: sc.emit_engine("vector", e, sems))
    block.gpsimd(lambda e: sc.emit_engine("gpsimd", e, sems))
    blk.__exit__(None, None, None)
    for g in reversed(semctx):
        g.__exit__(None, None, None)
    for g in reversed(ctx):
        g.__exit__(None, None, None)
    return nc


def prep_core_inputs(xi, w):
    d = {"x": np.ascontiguousarray(xi.T.reshape(8, 128, 2048).transpose(1, 0, 2))}
    d.update(w)
    return d


def prep_weights(inp):
    f32 = np.float32

    def featw(wT):
        return np.ascontiguousarray(wT.reshape(8, 128, 1024).transpose(1, 0, 2))

    def colz(v):
        return np.ascontiguousarray(np.asarray(v, f32).reshape(8, 128).T)

    w = {}
    w["wrF"] = featw(np.ascontiguousarray(np.asarray(inp["L0_ffn_W_r"], f32).T))
    w["wkF"] = featw(np.ascontiguousarray(np.asarray(inp["L0_ffn_W_k"], f32).T))
    w["wvF"] = featw(np.ascontiguousarray(np.asarray(inp["L0_ffn_W_v"], f32).T))
    wq2 = np.asarray(inp["L0_attn_W_q"], f32).transpose(1, 0, 2).reshape(1024, 1024)
    wk2 = np.asarray(inp["L0_attn_W_k"], f32).transpose(1, 0, 2).reshape(1024, 1024)
    wv2 = np.asarray(inp["L0_attn_W_v"], f32).transpose(1, 0, 2).reshape(1024, 1024)
    w["wq"] = np.ascontiguousarray(
        wq2.reshape(8, 128, 2, 4, 128).transpose(2, 1, 0, 3, 4))
    w["wkA"] = np.ascontiguousarray(wk2.reshape(8, 128, 2, 512).transpose(2, 1, 0, 3))
    w["wvA"] = np.ascontiguousarray(wv2.reshape(8, 128, 2, 512).transpose(2, 1, 0, 3))
    wo = np.asarray(inp["L0_attn_W_o"], f32).reshape(2, 8, 64, 8, 128)
    w["wo"] = np.ascontiguousarray(wo.transpose(0, 2, 3, 1, 4))
    w["mur"] = colz(inp["L0_ffn_mu_r"])
    w["muk"] = colz(inp["L0_ffn_mu_k"])
    w["bq"] = colz(np.asarray(inp["L0_attn_b_q"], f32).reshape(1024))
    w["bk"] = colz(np.asarray(inp["L0_attn_b_k"], f32).reshape(1024))
    w["bo"] = colz(inp["L0_attn_b_o"])
    w["lnw"] = colz(inp["L0_attn_ln_w"])
    w["lnb"] = colz(inp["L0_attn_ln_b"])
    w["bv"] = np.ascontiguousarray(np.asarray(inp["L0_attn_b_v"], f32).reshape(1, 1024))
    w["tri"] = np.ascontiguousarray(np.triu(np.ones((128, 128), np.float32)))
    return w


def kernel(**inputs):
    stage = inputs.pop("_stage", "full")
    nc = build(stage)
    w = prep_weights(inputs)
    x = np.asarray(inputs["L0_attn_resid_stream"], np.float32)
    in_maps = [prep_core_inputs(x[i], w) for i in range(8)]
    res = run_bass_kernel_spmd(nc, in_maps, core_ids=list(range(8)))
    outs = []
    for i in range(8):
        o = res.results[i]["out"]
        outs.append(o.transpose(1, 0, 2).reshape(1024, 2048).T)
    return np.stack(outs, axis=0).astype(np.float32)
